# version 07
# baseline (speedup 1.0000x reference)
import numpy as np
from contextlib import ExitStack

import concourse.bass as bass
import concourse.mybir as mybir
from concourse.bass_utils import run_bass_kernel_spmd

F32 = mybir.dt.float32
F16 = mybir.dt.float16
I32 = mybir.dt.int32
AF = mybir.ActivationFunctionType
ALU = mybir.AluOpType
AX = mybir.AxisListType

S = 2048
D = 1024
NB = 4
BT = 512
NT = 4
ND = 8
NE = 16
NI = 32
NO = 8
NW2U = 32
NUNITS = {"whg": NE, "w1": NI, "w2": NW2U}
MMKEY = {"whg": "hg", "w1": "f1", "w2": "f2u"}
EPS = 1e-5


class Ctr:

    def __init__(self):
        self.n = 0
        self.at = {}

    def ev(self, key, k=1):
        self.n += k
        self.at[key] = self.n
        return self.n


def plan():
    mm = Ctr()
    ve = Ctr()
    sc = Ctr()
    gp = Ctr()

    qwt = {}
    for name, nu in NUNITS.items():
        for b in range(NB):
            for u in range(nu):
                for h in range(2):
                    qwt[(name, b, u, h)] = 16 * (b * nu + u + 1)

    def tr_events(b):
        for k in range(NT):
            for h in range(2):
                mm.ev(("tr", b, k, h))

    tr_events(0)
    for b in range(NB):
        for j in range(NE):
            mm.ev(("hg", b, j))
        for i in range(NI):
            mm.ev(("f1", b, i))
        for u in range(NW2U):
            mm.ev(("f2u", b, u))
        for k in range(NT):
            for h in range(2):
                mm.ev(("ytr", b, k, h))
        if b + 1 < NB:
            tr_events(b + 1)

    for b in range(NB):
        for name, nu in NUNITS.items():
            for u in range(nu):
                for h in range(2):
                    gp.ev(("sp", name, b, u, h))

    ve.ev(("init",))

    def ln_events(b):
        for k in range(NT):
            ve.ev(("xn", b, k))

    ln_events(0)
    for b in range(NB):
        for k in range(NT):
            for h in range(2):
                ve.ev(("evac", b, k, h))
        for kk in range(ND):
            ve.ev(("lin", b, kk))
            ve.ev(("scan", b, kk))
        for i in range(NI):
            ve.ev(("h1lo", b, i))
        if b + 1 < NB:
            ln_events(b + 1)
        for o in range(NO):
            ve.ev(("fb", b, o))
            ve.ev(("g2", b, o))
            ve.ev(("y", b, o))
        for k in range(NT):
            ve.ev(("res", b, k))

    for b in range(NB):
        for kk in range(ND):
            sc.ev(("z", b, kk))
            sc.ev(("sigh", b, kk))
        for i in range(NI):
            sc.ev(("h1", b, i))
        for o in range(NO):
            sc.ev(("sf", b, o))

    return mm, ve, sc, gp, qwt


def build_nc():
    mm, ve, sc, gp, qwt = plan()
    MM, VE, SC, GP, QWT = mm.at, ve.at, sc.at, gp.at, qwt

    nc = bass.Bass("TRN2", target_bir_lowering=False,
                   detect_race_conditions=False)

    x_hbm = nc.dram_tensor("x", [S, D], F32, kind="ExternalInput").ap()
    whg_hbm = nc.dram_tensor("whg", [2 * NE, 128, 512], F32, kind="ExternalInput").ap()
    w1_hbm = nc.dram_tensor("w1", [2 * NI, 128, 512], F32, kind="ExternalInput").ap()
    w2_hbm = nc.dram_tensor("w2", [2 * NW2U, 128, 512], F32, kind="ExternalInput").ap()
    b1_hbm = nc.dram_tensor("b1t", [128, NI], F32, kind="ExternalInput").ap()
    b2_hbm = nc.dram_tensor("b2t", [128, NO], F32, kind="ExternalInput").ap()
    id_hbm = nc.dram_tensor("ident", [128, 128], F32, kind="ExternalInput").ap()
    out_hbm = nc.dram_tensor("out", [S, D], F32, kind="ExternalOutput").ap()
    whbm = {"whg": whg_hbm, "w1": w1_hbm, "w2": w2_hbm}

    es = ExitStack()
    with es:
        def sb(name, shape, dt=F32):
            return es.enter_context(nc.sbuf_tensor(name, shape, dt))

        def ps(name, shape):
            return es.enter_context(nc.psum_tensor(name, shape, F32))

        xt = [[sb(f"xt{p}_{k}", [128, D]) for k in range(NT)] for p in range(2)]
        xn_tm = [sb(f"xntm{k}", [128, D]) for k in range(NT)]
        xnhi = [sb(f"xnhi{d}", [128, BT], F16) for d in range(ND)]
        xnlo = [sb(f"xnlo{d}", [128, BT], F16) for d in range(ND)]
        h1hi = [sb(f"h1hi{i}", [128, BT], F16) for i in range(NI)]
        h1lo = [sb(f"h1lo{i}", [128, BT], F16) for i in range(NI)]
        h1f = [sb(f"h1f{p}", [128, BT]) for p in range(2)]
        s_t = [sb(f"s{d}", [128, BT]) for d in range(ND)]
        zbuf = [sb(f"z{p}", [128, BT]) for p in range(2)]
        gbuf = sb("gbuf", [128, BT])
        lbuf = sb("lbuf", [128, BT])
        vbuf = sb("vbuf", [128, BT])
        sfb = sb("sfb", [128, BT])
        g2b = sb("g2b", [128, BT])
        wf = {n: [sb(f"wf_{n}{h}", [128, 512]) for h in range(2)]
              for n in NUNITS}
        whi = {n: [sb(f"whi_{n}{p}", [128, 1024], F16) for p in range(2)]
               for n in NUNITS}
        wlo = {n: [sb(f"wlo_{n}{p}", [128, 1024], F16) for p in range(2)]
               for n in NUNITS}
        stats = sb("stats", [128, NT * 8])
        carry = sb("carry", [128, ND])
        neghalf = sb("neghalf", [128, 1])
        b1t = sb("b1t_s", [128, NI])
        b2t = sb("b2t_s", [128, NO])
        ident = sb("ident_s", [128, 128])

        pm = [ps("pm0", [128, BT]), ps("pm1", [128, BT])]
        pt = [ps("pt0", [128, BT]), ps("pt1", [128, BT])]
        py = [ps("py0", [128, BT]), ps("py1", [128, BT])]

        s_mm = es.enter_context(nc.semaphore("s_mm"))
        s_ve = es.enter_context(nc.semaphore("s_ve"))
        s_sc = es.enter_context(nc.semaphore("s_sc"))
        s_gp = es.enter_context(nc.semaphore("s_gp"))
        q_x = es.enter_context(nc.semaphore("q_x"))
        q_c = es.enter_context(nc.semaphore("q_c"))
        q_xk = [es.enter_context(nc.semaphore(f"q_xk{k}")) for k in range(NT)]
        q_w = {
            name: [es.enter_context(nc.semaphore(f"qw_{name}{h}")) for h in range(2)]
            for name in NUNITS
        }
        block = es.enter_context(nc.Block())

        def unit_seq():
            out = []
            for b in range(NB):
                for name, nu in NUNITS.items():
                    for u in range(nu):
                        out.append((name, b, u))
            return out

        UNITS = unit_seq()

        def grp_free(eng, b, seq):
            p = seq - 2
            if p < 0:
                return
            if p < NE:
                kk = p // 2
                if p % 2 == 0:
                    eng.wait_ge(s_sc, SC[("z", b, kk)])
                else:
                    eng.wait_ge(s_ve, VE[("lin", b, kk)])
            elif p < NE + NI:
                eng.wait_ge(s_sc, SC[("h1", b, p - NE)])
            else:
                eng.wait_ge(s_ve, VE[("fb", b, p - NE - NI)])

        @block.sync
        def _(sync):
            prev_per_name = {n: [] for n in NUNITS}
            for name, b, u in UNITS:
                hist = prev_per_name[name]
                for h in range(2):
                    if len(hist) >= 1:
                        pb, pu = hist[-1]
                        sync.wait_ge(s_gp, GP[("sp", name, pb, pu, h)])
                    sync.dma_start(
                        out=wf[name][h][:], in_=whbm[name][2 * u + h]
                    ).then_inc(q_w[name][h], 16)
                hist.append((b, u))

        @block.gpsimd
        def _(gpsimd):
            gpsimd.dma_start(out=ident[:], in_=id_hbm[:]).then_inc(q_c, 16)
            gpsimd.dma_start(out=b1t[:], in_=b1_hbm[:]).then_inc(q_c, 16)
            gpsimd.dma_start(out=b2t[:], in_=b2_hbm[:]).then_inc(q_c, 16)

            def xload(b, k):
                t0 = b * BT + k * 128
                gpsimd.dma_start(
                    out=xt[b % 2][k][:], in_=x_hbm[t0:t0 + 128, :]
                ).then_inc(q_xk[k], 16)

            def splits(b):
                for name, nu in NUNITS.items():
                    for u in range(nu):
                        for h in range(2):
                            gpsimd.wait_ge(q_w[name][h], QWT[(name, b, u, h)])
                            if h == 0:
                                g = b * nu + u - 2
                                if g >= 0:
                                    pb, pu = divmod(g, nu)
                                    gpsimd.wait_ge(
                                        s_mm, MM[(MMKEY[name], pb, pu)]
                                    )
                            dst_hi = whi[name][u % 2][:, h * 512:(h + 1) * 512]
                            dst_lo = wlo[name][u % 2][:, h * 512:(h + 1) * 512]
                            gpsimd.tensor_copy(dst_hi, wf[name][h][:])
                            gpsimd.drain()
                            gpsimd.tensor_tensor(
                                dst_lo, wf[name][h][:], dst_hi, ALU.subtract
                            )
                            gpsimd.drain()
                            gpsimd.sem_inc(s_gp, 1)

            for k in range(NT):
                xload(0, k)
            for k in range(NT):
                xload(1, k)
            for b in range(NB):
                if b >= 1:
                    for k in range(NT):
                        t0 = (b - 1) * BT + k * 128
                        gpsimd.wait_ge(s_ve, VE[("res", b - 1, k)])
                        gpsimd.dma_start(
                            out=out_hbm[t0:t0 + 128, :], in_=xt[(b - 1) % 2][k][:]
                        ).then_inc(q_x, 16)
                    if b + 1 < NB:
                        for k in range(NT):
                            xload(b + 1, k)
                splits(b)
            for k in range(NT):
                t0 = (NB - 1) * BT + k * 128
                gpsimd.wait_ge(s_ve, VE[("res", NB - 1, k)])
                gpsimd.dma_start(
                    out=out_hbm[t0:t0 + 128, :], in_=xt[(NB - 1) % 2][k][:]
                ).then_inc(q_x, 16)
            gpsimd.wait_ge(q_x, 16 * NB * NT)

        @block.tensor
        def _(tensor):
            tensor.wait_ge(q_c, 48)

            def tr_group(b):
                for k in range(NT):
                    slot = k
                    tensor.wait_ge(s_ve, VE[("xn", b, k)])
                    for h in range(2):
                        if k >= 1:
                            tensor.wait_ge(s_ve, VE[("evac", b, k - 1, h)])
                        elif b >= 1:
                            tensor.wait_ge(s_ve, VE[("evac", b - 1, NT - 1, h)])
                        last = None
                        for dd in range(4):
                            d = h * 4 + dd
                            last = tensor.matmul(
                                pt[h][:, dd * 128:(dd + 1) * 128],
                                xn_tm[slot][:, d * 128:(d + 1) * 128],
                                ident[:], start=True, stop=True,
                            )
                        last.then_inc(s_mm, 1)

            def passes3(bank, w_hi, w_lo, x_hi, x_lo, first, final):
                n = 0
                last = None
                for a, bb in ((w_hi, x_hi), (w_hi, x_lo), (w_lo, x_hi)):
                    last = tensor.matmul(
                        bank[:], a, bb,
                        start=(first and n == 0), stop=(final and n == 2),
                    )
                    n += 1
                return last

            tr_group(0)
            for b in range(NB):
                seq = 0
                for j in range(NE):
                    if j == 0:
                        tensor.wait_ge(s_ve, VE[("evac", b, NT - 1, 1)])
                    grp_free(tensor, b, seq)
                    bank = pm[seq % 2]
                    last = None
                    for d in range(ND):
                        if d == 0:
                            tensor.wait_ge(s_gp, GP[("sp", "whg", b, j, 0)])
                        elif d == 4:
                            tensor.wait_ge(s_gp, GP[("sp", "whg", b, j, 1)])
                        last = passes3(
                            bank,
                            whi["whg"][j % 2][:, d * 128:(d + 1) * 128],
                            wlo["whg"][j % 2][:, d * 128:(d + 1) * 128],
                            xnhi[d][:], xnlo[d][:],
                            first=(d == 0), final=(d == ND - 1),
                        )
                    last.then_inc(s_mm, 1)
                    seq += 1
                for i in range(NI):
                    grp_free(tensor, b, seq)
                    bank = pm[seq % 2]
                    last = None
                    for d in range(ND):
                        if d == 0:
                            tensor.wait_ge(s_gp, GP[("sp", "w1", b, i, 0)])
                        elif d == 4:
                            tensor.wait_ge(s_gp, GP[("sp", "w1", b, i, 1)])
                        last = passes3(
                            bank,
                            whi["w1"][i % 2][:, d * 128:(d + 1) * 128],
                            wlo["w1"][i % 2][:, d * 128:(d + 1) * 128],
                            xnhi[d][:], xnlo[d][:],
                            first=(d == 0), final=(d == ND - 1),
                        )
                    last.then_inc(s_mm, 1)
                    seq += 1
                for o in range(NO):
                    grp_free(tensor, b, seq)
                    bank = pm[seq % 2]
                    for u4 in range(4):
                        u = o * 4 + u4
                        tensor.wait_ge(s_sc, SC[("h1", b, u4 * 8 + 7)])
                        tensor.wait_ge(s_ve, VE[("h1lo", b, u4 * 8 + 7)])
                        last = None
                        for ii in range(8):
                            i = u4 * 8 + ii
                            if ii == 0:
                                tensor.wait_ge(s_gp, GP[("sp", "w2", b, u, 0)])
                            elif ii == 4:
                                tensor.wait_ge(s_gp, GP[("sp", "w2", b, u, 1)])
                            last = passes3(
                                bank,
                                whi["w2"][u % 2][:, ii * 128:(ii + 1) * 128],
                                wlo["w2"][u % 2][:, ii * 128:(ii + 1) * 128],
                                h1hi[i][:], h1lo[i][:],
                                first=(i == 0), final=(i == NI - 1),
                            )
                        last.then_inc(s_mm, 1)
                    seq += 1
                for k in range(NT):
                    for h in range(2):
                        tensor.wait_ge(s_ve, VE[("y", b, h * 4 + 3)])
                        if k >= 1:
                            tensor.wait_ge(s_ve, VE[("res", b, k - 1)])
                        elif b >= 1:
                            tensor.wait_ge(s_ve, VE[("res", b - 1, NT - 1)])
                        last = None
                        for dd in range(4):
                            d = h * 4 + dd
                            last = tensor.matmul(
                                py[h][:, dd * 128:(dd + 1) * 128],
                                s_t[d][:, k * 128:(k + 1) * 128],
                                ident[:], start=True, stop=True,
                            )
                        last.then_inc(s_mm, 1)
                if b + 1 < NB:
                    tr_group(b + 1)

        @block.scalar
        def _(scalar):
            scalar.wait_ge(q_c, 48)
            for b in range(NB):
                for kk in range(ND):
                    if kk >= 2:
                        scalar.wait_ge(s_ve, VE[("scan", b, kk - 2)])
                    elif b > 0:
                        scalar.wait_ge(s_ve, VE[("scan", b - 1, kk + 6)])
                    scalar.wait_ge(s_mm, MM[("hg", b, 2 * kk)])
                    gate_bank = pm[(2 * kk) % 2]
                    scalar.activation(
                        out=zbuf[kk % 2][:], in_=gate_bank[:], func=AF.Sigmoid,
                    ).then_inc(s_sc, 1)
                    scalar.wait_ge(s_ve, VE[("lin", b, kk)])
                    scalar.activation(
                        out=gbuf[:], in_=lbuf[:], func=AF.Sigmoid,
                        bias=neghalf[:], scale=1.0,
                    ).then_inc(s_sc, 1)
                for i in range(NI):
                    scalar.wait_ge(s_mm, MM[("f1", b, i)])
                    bank = pm[(NE + i) % 2]
                    scalar.activation(
                        out=h1hi[i][:], in_=bank[:], func=AF.Relu,
                        bias=b1t[:, i:i + 1], scale=1.0,
                    )
                    if i >= 2:
                        scalar.wait_ge(s_ve, VE[("h1lo", b, i - 2)])
                    elif b > 0:
                        scalar.wait_ge(s_ve, VE[("h1lo", b - 1, 30 + i)])
                    scalar.activation(
                        out=h1f[i % 2][:], in_=bank[:], func=AF.Relu,
                        bias=b1t[:, i:i + 1], scale=1.0,
                    ).then_inc(s_sc, 1)
                for o in range(NO):
                    scalar.wait_ge(s_ve, VE[("fb", b, o)])
                    scalar.activation(
                        out=sfb[:], in_=g2b[:], func=AF.Sigmoid,
                    ).then_inc(s_sc, 1)

        @block.vector
        def _(vector):
            def vdrain():
                vector.drain()

            vector.memset(carry[:], 0.0)
            vector.memset(neghalf[:], -0.5).then_inc(s_ve, 1)

            def ln_vector(b):
                for k in range(NT):
                    slot = k
                    vector.wait_ge(q_xk[k], 16 * (b + 1))
                    if b >= 1:
                        vector.wait_ge(s_mm, MM[("tr", b - 1, k, 1)])
                    base = k * 8
                    su = stats[:, base + 0:base + 1]
                    sq_ = stats[:, base + 1:base + 2]
                    mean = stats[:, base + 2:base + 3]
                    ex2 = stats[:, base + 3:base + 4]
                    w = stats[:, base + 4:base + 5]
                    rstd = stats[:, base + 6:base + 7]
                    tmp = stats[:, base + 7:base + 8]
                    vector.reduce_sum(su, xt[b % 2][k][:], axis=AX.X)
                    vector.scalar_tensor_tensor(
                        xn_tm[slot][:], xt[b % 2][k][:], 1.0, xt[b % 2][k][:],
                        ALU.mult, ALU.mult, accum_out=sq_,
                    )
                    vdrain()
                    vector.tensor_scalar_mul(mean, su, 1.0 / D)
                    vector.tensor_scalar_mul(ex2, sq_, 1.0 / D)
                    vdrain()
                    vector.tensor_tensor(tmp, mean, mean, ALU.mult)
                    vdrain()
                    vector.tensor_sub(w, ex2, tmp)
                    vdrain()
                    vector.tensor_scalar_add(w, w, EPS)
                    vdrain()
                    vector.tensor_single_scalar(
                        rstd.bitcast(I32), w.bitcast(I32), 1,
                        ALU.logical_shift_right,
                    )
                    vdrain()
                    vector.tensor_scalar(
                        rstd.bitcast(I32), rstd.bitcast(I32), -1, 0x5F3759DF,
                        ALU.mult, ALU.add,
                    )
                    vdrain()
                    for _ in range(5):
                        vector.tensor_mul(tmp, w, rstd)
                        vdrain()
                        vector.tensor_mul(tmp, tmp, rstd)
                        vdrain()
                        vector.tensor_scalar(tmp, tmp, -0.5, 1.5, ALU.mult, ALU.add)
                        vdrain()
                        vector.tensor_mul(rstd, rstd, tmp)
                        vdrain()
                    vector.tensor_scalar(
                        xn_tm[slot][:], xt[b % 2][k][:], mean, rstd,
                        ALU.subtract, ALU.mult,
                    ).then_inc(s_ve, 1)

            ln_vector(0)
            for b in range(NB):
                for k in range(NT):
                    for h in range(2):
                        vector.wait_ge(s_mm, MM[("tr", b, k, h)])
                        for dd in range(4):
                            d = h * 4 + dd
                            vector.tensor_copy(
                                xnhi[d][:, k * 128:(k + 1) * 128],
                                pt[h][:, dd * 128:(dd + 1) * 128],
                            )
                        vdrain()
                        last = None
                        for dd in range(4):
                            d = h * 4 + dd
                            last = vector.tensor_tensor(
                                xnlo[d][:, k * 128:(k + 1) * 128],
                                pt[h][:, dd * 128:(dd + 1) * 128],
                                xnhi[d][:, k * 128:(k + 1) * 128],
                                ALU.subtract,
                            )
                        last.then_inc(s_ve, 1)
                for kk in range(ND):
                    if b > 0 and kk == 0:
                        vector.wait_ge(s_mm, MM[("ytr", b - 1, NT - 1, 1)])
                    vector.wait_ge(s_mm, MM[("hg", b, 2 * kk + 1)])
                    hid_bank = pm[(2 * kk + 1) % 2]
                    vector.tensor_scalar_add(
                        lbuf[:], hid_bank[:], 0.5
                    ).then_inc(s_ve, 1)
                    vector.wait_ge(s_sc, SC[("sigh", b, kk)])
                    vdrain()
                    vector.tensor_max(lbuf[:], lbuf[:], gbuf[:])
                    vector.wait_ge(s_sc, SC[("z", b, kk)])
                    vdrain()
                    vector.tensor_mul(vbuf[:], zbuf[kk % 2][:], lbuf[:])
                    vector.tensor_scalar(
                        zbuf[kk % 2][:], zbuf[kk % 2][:], -1.0, 1.0, ALU.mult, ALU.add
                    )
                    vdrain()
                    vector.tensor_tensor_scan(
                        s_t[kk][:], zbuf[kk % 2][:], vbuf[:],
                        carry[:, kk:kk + 1], ALU.mult, ALU.add,
                    )
                    vdrain()
                    vector.tensor_copy(
                        carry[:, kk:kk + 1], s_t[kk][:, BT - 1:BT]
                    ).then_inc(s_ve, 1)
                for i in range(NI):
                    vector.wait_ge(s_sc, SC[("h1", b, i)])
                    vector.tensor_tensor(
                        h1lo[i][:], h1f[i % 2][:], h1hi[i][:], ALU.subtract
                    ).then_inc(s_ve, 1)
                if b + 1 < NB:
                    ln_vector(b + 1)
                for o in range(NO):
                    vector.wait_ge(s_mm, MM[("f2u", b, o * 4 + 3)])
                    bank = pm[(NE + NI + o) % 2]
                    vector.tensor_scalar_add(
                        g2b[:], bank[:], b2t[:, o:o + 1]
                    ).then_inc(s_ve, 1)
                    vector.wait_ge(s_sc, SC[("sf", b, o)])
                    vdrain()
                    vector.tensor_mul(g2b[:], g2b[:], sfb[:]).then_inc(s_ve, 1)
                    vdrain()
                    vector.tensor_mul(
                        s_t[o][:], s_t[o][:], g2b[:]
                    ).then_inc(s_ve, 1)
                for k in range(NT):
                    last = None
                    for h in range(2):
                        vector.wait_ge(s_mm, MM[("ytr", b, k, h)])
                        last = vector.tensor_tensor(
                            xt[b % 2][k][:, h * 512:(h + 1) * 512],
                            xt[b % 2][k][:, h * 512:(h + 1) * 512],
                            py[h][:], ALU.add,
                        )
                    last.then_inc(s_ve, 1)

    return nc


_NC_CACHE = None


def kernel(L0_attn_x, L0_attn_W_hg, L0_ffn_W1, L0_ffn_b1, L0_ffn_W2, L0_ffn_b2):
    global _NC_CACHE
    x = np.ascontiguousarray(np.asarray(L0_attn_x, dtype=np.float32))
    whg = np.asarray(L0_attn_W_hg, dtype=np.float32)
    w1 = np.asarray(L0_ffn_W1, dtype=np.float32)
    b1 = np.asarray(L0_ffn_b1, dtype=np.float32)
    w2 = np.asarray(L0_ffn_W2, dtype=np.float32)
    b2 = np.asarray(L0_ffn_b2, dtype=np.float32)

    def halves(units):
        n = units.shape[0]
        return np.ascontiguousarray(
            units.reshape(n, 128, 2, 512).transpose(0, 2, 1, 3).reshape(2 * n, 128, 512)
        )

    e_order = []
    for kk in range(8):
        e_order.extend([8 + kk, kk])
    whg_t = whg.reshape(8, 128, 16, 128).transpose(2, 1, 0, 3)[e_order]
    whg_t = halves(whg_t.reshape(16, 128, 1024))
    w1_t = halves(
        w1.reshape(8, 128, 32, 128).transpose(2, 1, 0, 3).reshape(32, 128, 1024)
    )
    w2_t = halves(
        w2.reshape(4, 8, 128, 8, 128).transpose(3, 0, 2, 1, 4).reshape(32, 128, 1024)
    )
    b1_t = np.ascontiguousarray(b1.reshape(32, 128).T)
    b2_t = np.ascontiguousarray(b2.reshape(8, 128).T)
    ident = np.eye(128, dtype=np.float32)

    if _NC_CACHE is None:
        _NC_CACHE = build_nc()
    nc = _NC_CACHE

    in_maps = [
        {
            "x": np.ascontiguousarray(x[c]),
            "whg": whg_t,
            "w1": w1_t,
            "w2": w2_t,
            "b1t": b1_t,
            "b2t": b2_t,
            "ident": ident,
        }
        for c in range(8)
    ]
    res = run_bass_kernel_spmd(nc, in_maps, core_ids=list(range(8)))
    return np.stack([r["out"] for r in res.results], axis=0)
